# version 25
# speedup vs baseline: 1.9306x; 1.0085x over previous
import contextlib

import numpy as np
import ml_dtypes
import concourse.bass as bass
import concourse.mybir as mb
from concourse.bass_utils import run_bass_kernel_spmd
from concourse.alu_op_type import AluOpType as OP

F32 = mb.dt.float32
F32R = mb.dt.float32r
BF16 = mb.dt.bfloat16
U8 = mb.dt.uint8
I32 = mb.dt.int32
AF = mb.ActivationFunctionType

NCORES = 8
B, L, DM = 2, 16384, 1024
TT = B * L
TC = TT // NCORES
DI = 2048
DSL = DI // NCORES
NDT = DSL // 128
NS = 16
DTR = 64
XDB = DTR + 2 * NS
WIN = 1024
NW = TT // WIN
GRP = 8
NG = NW // GRP
SEQWIN = L // WIN
WPO = TC // WIN
LN_EPS = 1e-5
RMS_EPS = 1e-5
MAGIC = float(np.float32(3 * 2**22))
CSPLIT = float(np.float32(2**22 + 1))
FH = 2816
FSH_GU = FH // NCORES
FSH_D = DM // NCORES
FSUBS = (1024, 1024, 768)
RACE_SAFE = False
SAFE_V = True
SAFE_S = False
SAFE_T = True
SAFE_TQ = True


class Sem:
    def __init__(self, es, nc, name):
        self.h = es.enter_context(nc.semaphore(name))
        self.n = 0

    def inc(self, k=1):
        self.n += k
        return self.n


def build_nc():
    nc = bass.Bass("TRN2", target_bir_lowering=False,
                   detect_race_conditions=RACE_SAFE)
    es = contextlib.ExitStack()

    def din(name, shape, dt=F32):
        return nc.dram_tensor(name, shape, dt, kind="ExternalInput").ap()

    x_in = din("x", [TC, DM])
    normw_in = din("normw", [DM])
    wipT_in = din("wipT", [DM, 2 * DSL])
    convw_in = {d: din(f"convw_{d}", [DSL, 4]) for d in "fb"}
    convb_in = {d: din(f"convb_{d}", [DSL]) for d in "fb"}
    xpjT_in = {d: din(f"xpjT_{d}", [DSL, XDB]) for d in "fb"}
    dtwT_in = {d: din(f"dtwT_{d}", [DTR, DSL]) for d in "fb"}
    dtb_in = {d: din(f"dtb_{d}", [DSL]) for d in "fb"}
    alog_in = {d: din(f"alog_{d}", [DSL, NS]) for d in "fb"}
    dvec_in = {d: din(f"dvec_{d}", [DSL]) for d in "fb"}
    wopT_in = din("wopT", [DI, DM])
    wg_in = din("wg", [FSH_GU, DM])
    wu_in = din("wu", [FSH_GU, DM])
    wd_in = din("wd", [FSH_D, FH])
    cid_in = din("cid", [1, 1], U8 if False else mb.dt.uint32)
    out_ext = nc.dram_tensor("out", [TC, DM], F32, kind="ExternalOutput").ap()

    def dint(name, shape, dt, shared=False):
        if shared:
            return nc.dram_tensor(name, shape, dt, addr_space="Shared")
        return nc.dram_tensor(name, shape, dt)

    xn1_s = dint("xn1_s", [TC, DM], F32)
    xqT_s = dint("xqT_s", [DM, TC], BF16)
    g_s = dint("g_s", [TC, DM], BF16)
    agx_in = dint("agx_in", [(TC // 1024) * DM, 1024], BF16)
    agx_out = dint("agx_out", [(TC // 1024) * NCORES * DM, 1024], BF16, shared=True)
    z_sp = dint("z_sp", [DSL, TT], BF16)
    xi_sp = dint("xi_sp", [DSL, TT], BF16)
    xc_sp = {d: dint(f"xc_sp_{d}", [DSL, TT], BF16) for d in "fb"}
    ff_sp = dint("ff_sp", [DSL, TT], BF16)
    ar_in = [dint(f"ar_in{g}", [2 * XDB, GRP * WIN], F32) for g in range(NG)]
    ar_out = [dint(f"ar_out{g}", [2 * XDB, GRP * WIN], F32,
                   shared=True) for g in range(NG)]
    ags_in = dint("ags_in", [(TT // TC) * DSL, TC], BF16)
    ags_out = dint("ags_out", [(TT // TC) * NCORES * DSL, TC], BF16, shared=True)
    agwg_in = dint("agwg_in", [DM, FSH_GU], BF16)
    agwg_out = dint("agwg_out", [NCORES * DM, FSH_GU], BF16, shared=True)
    agwu_in = dint("agwu_in", [DM, FSH_GU], BF16)
    agwu_out = dint("agwu_out", [NCORES * DM, FSH_GU], BF16, shared=True)
    agwd_in = dint("agwd_in", [FH, FSH_D], BF16)
    agwd_out = dint("agwd_out", [NCORES * FH, FSH_D], BF16, shared=True)

    def sb(name, shape, dt):
        return es.enter_context(nc.sbuf_tensor(name, shape, dt))

    identb = sb("identb", [128, 128], BF16)
    ones_row = sb("ones_row", [1, 128], BF16)
    rowi = sb("rowi", [128, 128], I32)
    coli = sb("coli", [128, 128], I32)
    masku = sb("masku", [128, 128], U8)
    zerb = sb("zerb", [128, 128], BF16)
    onebP = sb("onebP", [128, 128], BF16)
    nw8 = sb("nw8", [128, 8], F32)
    cw_b16 = {d: sb(f"cw_b16_{d}", [128, NDT * 4], BF16) for d in "fb"}
    cdiag = {d: sb(f"cdiag_{d}", [128, NDT * 4 * 128], F32R) for d in "fb"}
    cdiag_st = sb("cdiag_st", [128, 128], F32)
    cw_f32 = {d: sb(f"cw_f32_{d}", [128, NDT * 4], F32) for d in "fb"}
    zerf = sb("zerf", [128, 128], F32)
    xiwf = es.enter_context(nc.sbuf_tensor("xiwf", [128, NDT, WIN + 6], F32R))
    convb_c = {d: sb(f"convb_c_{d}", [128, NDT], F32) for d in "fb"}
    negdtb_c = {d: sb(f"negdtb_c_{d}", [128, NDT], F32) for d in "fb"}
    dvec_c = {d: sb(f"dvec_c_{d}", [128, NDT], F32) for d in "fb"}
    aexp_c = {d: sb(f"aexp_c_{d}", [128, NDT * NS], F32) for d in "fb"}
    xpjT_c = {d: sb(f"xpjT_c_{d}", [128, NDT, XDB], BF16) for d in "fb"}
    dtwT_c = {d: sb(f"dtwT_c_{d}", [DTR, NDT * 128], BF16) for d in "fb"}
    wip = sb("wip", [128, 8, 2 * DSL], BF16)
    nind = sb("nind", [NS, NS * 128], BF16)
    nmask = sb("nmask", [NS, 128], U8)
    carry = {d: sb(f"carry_{d}", [128, NDT * NS], F32) for d in "fb"}
    scid = sb("scid", [1, 1], mb.dt.uint32)
    epsc = sb("epsc", [128, 1], F32)
    absm = sb("absm", [128, 1], I32)

    poolF = sb("poolF", [128, 11264], F32)
    stage = poolF[:, 0:4096].rearrange("p (kc m) -> p kc m", m=512)
    qxn = poolF[:, 0:FH]
    qc = poolF[:, FH:2 * FH]
    qd = poolF[:, 2 * FH:3 * FH]
    qhi = poolF[:, 3 * FH:4 * FH]
    sigw = poolF[:, 0:WIN]
    lnsig = poolF[:, WIN:WIN + NDT * WIN].rearrange("p (dt w) -> p dt w", w=WIN)
    fw2 = poolF[:, 3 * WIN:5 * WIN].rearrange("p (dt w) -> p dt w", w=WIN)
    lnsigB = poolF[:, 5 * WIN:7 * WIN].rearrange("p (dt w) -> p dt w", w=WIN)
    x96v = poolF[0:XDB, 7 * WIN:8 * WIN]

    tx = sb("tx", [128, DM], F32)
    txn = sb("txn", [128, DM], F32)
    tb1 = sb("tb1", [128, DM], BF16)
    tb2 = sb("tb2", [128, 128], BF16)
    col1 = sb("col1", [128, 1], F32)
    col2 = sb("col2", [128, 1], F32)
    col3 = sb("col3", [128, 1], F32)
    col4 = sb("col4", [128, 1], F32)

    qmax = sb("qmax", [128, FH // 16], F32)
    qrs = sb("qrs", [128, FH // 16], F32)
    qsc = sb("qsc", [128, FH // 16], F32)
    qm = sb("qm", [128, FH], U8)
    qout = sb("qout", [128, FH], BF16)
    wq_st = sb("wq_st", [128, FH], F32)

    poolB = sb("poolB", [128, 46080], BF16)
    o = [0]

    def pview(n):
        v = poolB[:, o[0]:o[0] + n]
        o[0] += n
        return v

    xn2m = pview(8 * 512).rearrange("p (kc m) -> p kc m", m=512)
    xcw = pview(NDT * WIN).rearrange("p (dt w) -> p dt w", w=WIN)
    xcw2 = pview(NDT * WIN).rearrange("p (dt w) -> p dt w", w=WIN)
    xist = pview(NDT * (WIN + 6)).rearrange("p (dt w) -> p dt w", w=WIN + 6)
    dtmv2 = pview(2 * WIN).rearrange("p (pb w) -> p pb w", w=WIN)
    bcrowv2 = pview(2 * WIN).rearrange("p (pb w) -> p pb w", w=WIN)
    ccrowv2 = pview(2 * WIN).rearrange("p (pb w) -> p pb w", w=WIN)
    bbc2 = pview(2 * WIN).rearrange("p (sl w) -> p sl w", w=WIN)
    cbc2 = pview(2 * WIN).rearrange("p (sl w) -> p sl w", w=WIN)
    uw2 = pview(2 * NDT * WIN).rearrange("p (pb dt w) -> p pb dt w",
                                         dt=NDT, w=WIN)
    rw2 = pview(NDT * WIN).rearrange("p (dt w) -> p dt w", w=WIN)
    hw2 = pview(NDT * WIN).rearrange("p (dt w) -> p dt w", w=WIN)
    qw2 = pview(NDT * WIN).rearrange("p (dt w) -> p dt w", w=WIN)
    fullw = pview(NDT * WIN).rearrange("p (dt w) -> p dt w", w=WIN)
    ffw2 = pview(2 * NDT * WIN).rearrange("p (pb dt w) -> p pb dt w",
                                          dt=NDT, w=WIN)
    zw2 = pview(2 * NDT * WIN).rearrange("p (pb dt w) -> p pb dt w",
                                         dt=NDT, w=WIN)
    szw = pview(WIN)
    smixw = pview(WIN)
    SCAN_B_COLS = o[0]
    o[0] = 0
    wgu = pview(8 * 2048).rearrange("p (kc m) -> p kc m", m=2048)
    xqTt2 = pview(2 * 8 * 128).rearrange("p (sl kc m) -> p sl kc m",
                                         kc=8, m=128)
    hgt4 = pview(4 * FH).rearrange("p (q f) -> p q f", f=FH)
    hup = pview(FH)
    hqT = pview(22 * 128).rearrange("p (fc m) -> p fc m", m=128)
    wdm4b = pview(4 * DM)
    wdm4 = wdm4b.rearrange("p (sl ob m) -> p sl ob m", ob=8, m=128)
    wdm4f = wdm4b.rearrange("p (sl m) -> p sl m", m=DM)
    gfin = pview(DM)
    assert o[0] <= 46080
    wop16 = poolB[:, 0:16 * DM].rearrange("p (kc m) -> p kc m", m=DM)
    smv2 = poolB[:, 16 * DM:32 * DM].rearrange("p (sl kc m) -> p sl kc m",
                                               kc=16, m=512)
    gf2 = poolB[:, 32 * DM:34 * DM].rearrange("p (sl m) -> p sl m", m=DM)
    txn2 = poolB[:, 34 * DM:38 * DM].bitcast(F32).rearrange(
        "p (sl m) -> p sl m", m=DM)
    tx2 = poolB[:, 38 * DM:42 * DM].bitcast(F32).rearrange(
        "p (sl m) -> p sl m", m=DM)

    sfin = sb("sfin", [128, 2 * DM], F32).rearrange(
        "p (sl m) -> p sl m", m=DM)

    pA0 = es.enter_context(nc.psum_tensor("pA0", [128, 1024], F32))
    pA1 = es.enter_context(nc.psum_tensor("pA1", [128, 512], F32))
    pB = es.enter_context(nc.psum_tensor("pB", [128, 2048], F32))
    pT = es.enter_context(nc.psum_tensor("pT", [128, 128], BF16))

    s_v = Sem(es, nc, "s_v")
    s_s = Sem(es, nc, "s_s")
    s_t = Sem(es, nc, "s_t")
    NLD, NST, NGP, NSC = 10, 8, 6, 2

    class Pool:

        def __init__(self, name, k, eng):
            self.slots = [Sem(es, nc, f"{name}{i}") for i in range(k)]
            self.i = 0
            self.eng = eng

        def next_slot(self):
            s = self.slots[self.i]
            self.i = (self.i + 1) % len(self.slots)
            return s

        def tokens(self):
            return [(s, s.n) for s in self.slots if s.n > 0]

    p_ld = Pool("pld", NLD, "sync")
    p_st = Pool("pst", NST, "sync")
    p_gp = Pool("pgp", NGP, "gpsimd")
    p_sc = Pool("psc", NSC, "scalar")
    s_g = Sem(es, nc, "s_g")
    cc_sems = [Sem(es, nc, f"cc{i}") for i in range(20)]
    cc_i = [0]
    ALL_SEMS = ([s_v, s_s, s_t, s_g] + p_ld.slots + p_st.slots + p_gp.slots +
                p_sc.slots + cc_sems)
    ENGS = ("sync", "gpsimd", "vector", "scalar", "tensor")
    streams = {e: [] for e in ENGS}

    def em(eng, fn):
        streams[eng].append(fn)

    def wait(eng, sem, val=None):
        v = sem.n if val is None else val
        if v > 0:
            em(eng, lambda e, h=sem.h, v=v: e.wait_ge(h, v))

    def fence():
        vals = [(s.h, s.n) for s in ALL_SEMS if s.n > 0]
        for eng in ENGS:
            for h, v in vals:
                em(eng, lambda e, h=h, v=v: e.wait_ge(h, v))

    def wait_tokens(eng, toks):
        for s, v in toks:
            if v > 0:
                em(eng, lambda e, h=s.h, v=v: e.wait_ge(h, v))

    def wait_pool(eng, pool):
        wait_tokens(eng, pool.tokens())

    def _pool_dma(pool, dst, src, slow=False):
        sl_ = pool.next_slot()
        if sl_.n > 0:
            em(pool.eng, lambda e, h=sl_.h, v=sl_.n: e.wait_ge(h, v))
        em(pool.eng, lambda e, d=dst, s=src, sl=slow, h=sl_.h: e.dma_start(
            out=d, in_=s, allow_slow_non_contiguous=sl).then_inc(h, 16))
        sl_.inc(16)
        return (sl_, sl_.n)

    def ld(dst, src, slow=False):
        return _pool_dma(p_ld, dst, src, slow)

    def gdma(dst, src):
        return _pool_dma(p_gp, dst, src)

    def st(dst, src):
        return _pool_dma(p_st, dst, src)

    def wait_tok(eng, tok):
        s, v = tok
        em(eng, lambda e, h=s.h, v=v: e.wait_ge(h, v))

    def cc_op(fn):
        s = cc_sems[cc_i[0]]
        cc_i[0] += 1
        em("gpsimd", lambda e, f=fn, h=s.h: f(e).then_inc(h, 1))
        s.inc()
        return s

    def v_sp(fn, dep=None):
        d = s_v.n if dep is None else dep
        if d > 0:
            em("vector", lambda e, v=d: e.wait_ge(s_v.h, v))
        em("vector", lambda e: fn(e))
        em("vector", lambda e: e.drain().then_inc(s_v.h, 1))
        return s_v.inc()

    def v_op(fn, dep=None):
        d = s_v.n if dep is None else dep
        if d > 0:
            em("vector", lambda e, v=d: e.wait_ge(s_v.h, v))
        em("vector", lambda e: fn(e).then_inc(s_v.h, 1))
        return s_v.inc()

    def s_op(fn, dep=None):
        d = s_s.n if dep is None else dep
        if d > 0:
            em("scalar", lambda e, v=d: e.wait_ge(s_s.h, v))
        em("scalar", lambda e: fn(e).then_inc(s_s.h, 1))
        return s_s.inc()

    def t_op(fn):
        if SAFE_T and s_t.n > 0:
            em("tensor", lambda e, v=s_t.n: e.wait_ge(s_t.h, v))
        em("tensor", lambda e: fn(e).then_inc(s_t.h, 1))
        return s_t.inc()

    def t_quiet(fn):
        if SAFE_TQ:
            return t_op(fn)
        em("tensor", lambda e: fn(e))

    def pe_transpose(src_ap, rows, dst_sbuf_ap):
        wait("tensor", s_v)
        td = t_op(lambda e: e.transpose(pT[0:128, 0:rows], src_ap,
                                        identb[0:rows, 0:rows]))
        wait("vector", s_t)
        wait_pool("vector", p_st)
        return v_op(lambda e: e.tensor_copy(out=dst_sbuf_ap,
                                            in_=pT[0:128, 0:rows]))

    ld(nw8[:], normw_in.rearrange("(kc p) -> p kc", p=128), slow=True)
    ld(scid[:], cid_in)
    for d in "fb":
        ld(convb_c[d][:], convb_in[d].rearrange("(dt p) -> p dt", p=128), slow=True)
        ld(negdtb_c[d][:], dtb_in[d].rearrange("(dt p) -> p dt", p=128), slow=True)
        ld(dvec_c[d][:], dvec_in[d].rearrange("(dt p) -> p dt", p=128), slow=True)
        ld(aexp_c[d][:].rearrange("p (dt n) -> p dt n", n=NS),
           alog_in[d].rearrange("(dt p) n -> p dt n", p=128))
    ld_small_toks = p_ld.tokens()

    em("gpsimd", lambda e: e.iota(rowi[:], pattern=[[0, 128]], base=0,
                                  channel_multiplier=1))
    em("gpsimd", lambda e: e.iota(coli[:], pattern=[[1, 128]], base=0,
                                  channel_multiplier=0))
    em("gpsimd", lambda e: e.memset(zerb[:], 0.0))
    em("gpsimd", lambda e: e.memset(onebP[:], 1.0))
    em("gpsimd", lambda e: e.memset(ones_row[:], 1.0))
    em("gpsimd", lambda e: e.memset(carry["f"][:], 0.0))
    em("gpsimd", lambda e: e.memset(epsc[:], LN_EPS))
    em("gpsimd", lambda e: e.memset(absm[:], 2147483647))
    em("gpsimd", lambda e: e.memset(zerf[:], 0.0))
    em("gpsimd", lambda e: e.memset(carry["b"][:], 0.0))
    em("gpsimd", lambda e: e.drain().then_inc(s_g.h, 1))
    s_g.inc(1)

    wait("vector", s_g)
    v_op(lambda e: e.tensor_tensor(out=masku[:], in0=rowi[:],
                                   in1=coli[:], op=OP.is_equal))
    wait("vector", s_v)
    v_sp(lambda e: e.select(out=identb[:], mask=masku[:], on_true=onebP[:],
                            on_false=zerb[:], add_drain=True))
    wait("vector", s_v)
    for n in range(NS):
        v_op(lambda e, n=n: e.tensor_scalar(
            out=nmask[:], in0=rowi[0:NS, :], scalar1=float(n), scalar2=None,
            op0=OP.is_equal))
        wait("vector", s_v)
        v_sp(lambda e, n=n: e.select(out=nind[:, n * 128:(n + 1) * 128],
                                     mask=nmask[:], on_true=onebP[0:NS, :],
                                     on_false=zerb[0:NS, :], add_drain=True))
        wait("vector", s_v)
    for d in "fb":
        wait("sync", s_v)
        l = ld(cw_f32[d][:].rearrange("p (dt k) -> p dt k", k=4),
               convw_in[d].rearrange("(dt p) k -> p dt k", p=128))
        wait_tok("vector", l)
        for j in range(NDT * 4):
            wait("vector", s_v)
            v_sp(lambda e, d=d, j=j: e.select(
                out=cdiag_st[:], mask=masku[:],
                on_true=cw_f32[d][:, j:j + 1].broadcast_to([128, 128]),
                on_false=zerf[:], add_drain=True))
            wait("vector", s_v)
            v_op(lambda e, d=d, j=j: e.tensor_copy(
                out=cdiag[d][:, j * 128:(j + 1) * 128], in_=cdiag_st[:]))
        wait("sync", s_v)
        l = ld(stage[:, 1, 0:NDT * XDB].rearrange("p (dt e) -> p dt e", e=XDB),
               xpjT_in[d].rearrange("(dt p) e -> p dt e", p=128))
        wait_tok("vector", l)
        v_op(lambda e, d=d: e.tensor_copy(
            out=xpjT_c[d][:].rearrange("p dt e -> p (dt e)"),
            in_=stage[:, 1, 0:NDT * XDB]))
        wait("sync", s_v)
        l = ld(stage[0:DTR, 2, 0:NDT * 128], dtwT_in[d])
        wait_tok("vector", l)
        v_op(lambda e, d=d: e.tensor_copy(out=dtwT_c[d][:],
                                          in_=stage[0:DTR, 2, 0:NDT * 128]))
    wait_tokens("scalar", ld_small_toks)
    for d in "fb":
        s_op(lambda e, d=d: e.activation(out=aexp_c[d][:], in_=aexp_c[d][:],
                                         func=AF.Exp))
    wait_tokens("vector", ld_small_toks)
    for d in "fb":
        v_op(lambda e, d=d: e.tensor_scalar(out=negdtb_c[d][:],
                                            in0=negdtb_c[d][:], scalar1=-1.0,
                                            scalar2=None, op0=OP.mult))
    wait("sync", s_v)
    l = ld(stage[:], wipT_in.rearrange("(kc p) m -> p kc m", p=128))
    wait_tok("vector", l)
    for kc in range(8):
        v_op(lambda e, kc=kc: e.tensor_scalar(
            out=wip[:, kc, :], in0=stage[:, kc, :], scalar1=nw8[:, kc:kc + 1],
            scalar2=None, op0=OP.mult))
    fence()

    def quant_chain(width, rows, src_f32, dst_bf16):
        nb = width // 16
        v_sp(lambda e: e.tensor_reduce(
            out=qmax[0:rows, 0:nb],
            in_=src_f32.rearrange("p (g k) -> p g k", k=16),
            axis=mb.AxisListType.X, op=OP.max, apply_absolute_value=True))
        v_op(lambda e: e.tensor_scalar(out=qsc[0:rows, 0:nb],
                                       in0=qmax[0:rows, 0:nb],
                                       scalar1=1.0 / 6.0, scalar2=1e-12,
                                       op0=OP.mult, op1=OP.max))
        v_sp(lambda e: e.reciprocal(out=qrs[0:rows, 0:nb],
                                    in_=qsc[0:rows, 0:nb]))
        v_op(lambda e: e.tensor_tensor(
            out=qxn[0:rows, 0:width].rearrange("p (g k) -> p g k", k=16),
            in0=src_f32.rearrange("p (g k) -> p g k", k=16),
            in1=qrs[0:rows, 0:nb].unsqueeze(-1)
            .broadcast_to([rows, nb, 16]),
            op=OP.mult))
        v_op(lambda e: e.tensor_scalar(out=qc[0:rows, 0:width],
                                       in0=qxn[0:rows, 0:width],
                                       scalar1=CSPLIT, scalar2=None,
                                       op0=OP.mult))
        v_op(lambda e: e.tensor_tensor(out=qd[0:rows, 0:width],
                                       in0=qc[0:rows, 0:width],
                                       in1=qxn[0:rows, 0:width],
                                       op=OP.subtract))
        v_op(lambda e: e.tensor_tensor(out=qhi[0:rows, 0:width],
                                       in0=qc[0:rows, 0:width],
                                       in1=qd[0:rows, 0:width],
                                       op=OP.subtract))
        v_op(lambda e: e.tensor_scalar(out=qc[0:rows, 0:width],
                                       in0=qxn[0:rows, 0:width], scalar1=2.0,
                                       scalar2=MAGIC, op0=OP.mult, op1=OP.add))
        v_op(lambda e: e.tensor_scalar(out=qc[0:rows, 0:width],
                                       in0=qc[0:rows, 0:width], scalar1=MAGIC,
                                       scalar2=0.5, op0=OP.subtract,
                                       op1=OP.mult))
        v_op(lambda e: e.tensor_scalar(out=qd[0:rows, 0:width].bitcast(I32),
                                       in0=qxn[0:rows, 0:width].bitcast(I32),
                                       scalar1=absm[0:rows, :], scalar2=None,
                                       op0=OP.bitwise_and))
        v_op(lambda e: e.tensor_scalar(out=qm[0:rows, 0:width],
                                       in0=qd[0:rows, 0:width], scalar1=1.0,
                                       scalar2=None, op0=OP.is_ge))
        wait("vector", s_v)
        v_sp(lambda e: e.select(out=qxn[0:rows, 0:width],
                                mask=qm[0:rows, 0:width],
                                on_true=qhi[0:rows, 0:width],
                                on_false=qc[0:rows, 0:width], add_drain=True))
        wait("vector", s_v)
        return v_op(lambda e: e.tensor_tensor(
            out=dst_bf16.rearrange("p (g k) -> p g k", k=16),
            in0=qxn[0:rows, 0:width].rearrange("p (g k) -> p g k", k=16),
            in1=qsc[0:rows, 0:nb].unsqueeze(-1)
            .broadcast_to([rows, nb, 16]),
            op=OP.mult))

    for (w_src, ag_t) in ((wg_in, agwg_in), (wu_in, agwu_in)):
        for rt in range(3):
            r0 = rt * 128
            rows = min(128, FSH_GU - r0)
            wait("sync", s_v)
            wait("sync", s_s)
            l = ld(wq_st[0:rows, 0:DM], w_src[r0:r0 + rows, :])
            wait_tok("vector", l)
            wait_tok("scalar", l)
            quant_chain(DM, rows, wq_st[0:rows, 0:DM], qout[0:rows, 0:DM])
            for fb in range(8):
                vd = pe_transpose(qout[0:rows, fb * 128:(fb + 1) * 128], rows,
                                  tb2[0:128, 0:rows])
                wait("sync", s_v, vd)
                st(ag_t.ap()[fb * 128:(fb + 1) * 128, r0:r0 + rows],
                   tb2[0:128, 0:rows])
    wait("sync", s_v)
    wait("sync", s_s)
    l = ld(wq_st[:], wd_in)
    wait_tok("vector", l)
    wait_tok("scalar", l)
    quant_chain(FH, 128, wq_st[:], qout[:])
    for fb in range(22):
        vd = pe_transpose(qout[:, fb * 128:(fb + 1) * 128], 128,
                          tb2[0:128, 0:128])
        wait("sync", s_v, vd)
        st(agwd_in.ap()[fb * 128:(fb + 1) * 128, :], tb2[0:128, 0:128])

    wait_pool("gpsimd", p_st)
    for (ain, aout) in ((agwg_in, agwg_out), (agwu_in, agwu_out),
                        (agwd_in, agwd_out)):
        cc_op(lambda e, a=ain, o2=aout: e.collective_compute(
            "AllGather", OP.bypass, replica_groups=[list(range(NCORES))],
            ins=[a.ap().opt()], outs=[o2.ap().opt()]))
    fence()

    agx_cc = []
    txB = poolF[:, 0:DM]
    TXP = [tx, txB]
    ph0_rd = [0, 0]
    ph0_tok = [None, None]
    ph0_tp = [0]
    ph0_sttok = [None]

    def ph0_load(tt_i):
        par = tt_i % 2
        wait("sync", s_v, ph0_rd[par])
        ph0_tok[par] = ld(TXP[par][:],
                          x_in[tt_i * 128:(tt_i + 1) * 128, :])

    ph0_load(0)
    for tt_i in range(TC // 128):
        par = tt_i % 2
        txc = TXP[par]
        if tt_i + 1 < TC // 128:
            ph0_load(tt_i + 1)
        wait_tok("vector", ph0_tok[par])
        wait_tok("scalar", ph0_tok[par])
        wait("scalar", s_t, ph0_tp[0])
        v_sp(lambda e, txc=txc: e.tensor_reduce(out=col1[:], in_=txc[:],
                                                axis=mb.AxisListType.X,
                                                op=OP.add))
        sd = s_op(lambda e, txc=txc: e.activation(out=tb1[:], in_=txc[:],
                                                  func=AF.Square,
                                                  accum_out=col2[:]))
        wait("vector", s_s, sd)
        v_op(lambda e: e.tensor_scalar(out=col1[:], in0=col1[:],
                                       scalar1=1.0 / DM, scalar2=None,
                                       op0=OP.mult))
        v_op(lambda e: e.tensor_scalar(out=col3[:], in0=col2[:],
                                       scalar1=1.0 / DM, scalar2=None,
                                       op0=OP.mult))
        v_op(lambda e: e.tensor_scalar(out=col4[:], in0=col1[:],
                                       scalar1=col1[:], scalar2=None,
                                       op0=OP.mult))
        vv = v_op(lambda e: e.tensor_tensor(out=col3[:], in0=col3[:],
                                            in1=col4[:], op=OP.subtract))
        wait("scalar", s_v, vv)
        sd = s_op(lambda e: e.activation(out=col4[:], in_=col3[:],
                                         func=AF.Sqrt, bias=epsc[:]))
        wait("vector", s_s, sd)
        v_sp(lambda e: e.reciprocal(out=col4[:], in_=col4[:]))
        v_op(lambda e: e.tensor_scalar(out=col1[:], in0=col1[:],
                                       scalar1=col4[:], scalar2=None,
                                       op0=OP.mult))
        if ph0_sttok[0] is not None:
            wait_tokens("vector", [ph0_sttok[0]])
        vv = v_op(lambda e, txc=txc: e.tensor_scalar(
            out=txn[:], in0=txc[:], scalar1=col4[:], scalar2=col1[:],
            op0=OP.mult, op1=OP.subtract))
        ph0_rd[par] = vv
        wait("sync", s_v, vv)
        ph0_sttok[0] = st(xn1_s.ap()[tt_i * 128:(tt_i + 1) * 128, :],
                          txn[:])
        v_op(lambda e: e.tensor_scalar(out=col2[:], in0=col3[:],
                                       scalar1=LN_EPS, scalar2=None,
                                       op0=OP.add))
        v_sp(lambda e: e.reciprocal(out=col2[:], in_=col2[:]))
        vv = v_op(lambda e: e.tensor_scalar(out=col2[:], in0=col3[:],
                                            scalar1=col2[:], scalar2=RMS_EPS,
                                            op0=OP.mult, op1=OP.add))
        wait("scalar", s_v, vv)
        sd = s_op(lambda e: e.activation(out=col2[:], in_=col2[:],
                                         func=AF.Sqrt))
        wait("vector", s_s, sd)
        v_sp(lambda e: e.reciprocal(out=col2[:], in_=col2[:]))
        v_op(lambda e: e.tensor_scalar(out=tb1[:], in0=txn[:],
                                       scalar1=col2[:], scalar2=None,
                                       op0=OP.mult))
        for fb in range(8):
            vd = pe_transpose(tb1[:, fb * 128:(fb + 1) * 128], 128,
                              tb2[0:128, 0:128])
            wait("sync", s_v, vd)
            ch = tt_i // 8
            st(agx_in.ap()[ch * DM + fb * 128:ch * DM + (fb + 1) * 128,
                           (tt_i % 8) * 128:(tt_i % 8 + 1) * 128],
               tb2[0:128, 0:128])
        ph0_tp[0] = s_t.n
        if tt_i % 8 == 7:
            ch = tt_i // 8
            wait_pool("gpsimd", p_st)
            agx_cc.append(cc_op(lambda e, ch=ch: e.collective_compute(
                "AllGather", OP.bypass, replica_groups=[list(range(NCORES))],
                ins=[agx_in.ap()[ch * DM:(ch + 1) * DM, :].opt()],
                outs=[agx_out.ap()[ch * NCORES * DM:
                                   (ch + 1) * NCORES * DM, :].opt()])))

    def agx_win(w, c0, c1):
        blk = (w * WIN) // TC
        ch = (w * WIN) % TC // 1024
        base = ch * NCORES * DM + blk * DM
        return agx_out.ap()[base:base + DM, c0:c1]

    xn2mB = poolB[:, 24588:24588 + 4096].rearrange(
        "p (kc m) -> p kc m", m=512)
    XN2 = [xn2m, xn2mB]
    p2a_rd = [0, 0]
    p2a_tok = [None, None]

    P2A_ORDER = [blk * 4 + ch for ch in range(TC // 1024)
                 for blk in range(NCORES)]
    p2a_ch = [-1]

    def p2a_load(stp):
        wi_, sub_ = divmod(stp, WIN // 512)
        w_ = P2A_ORDER[wi_]
        par = stp % 2
        ch = (w_ * WIN) % TC // 1024
        if ch != p2a_ch[0]:
            ccs = agx_cc[ch]
            em("sync", lambda e, h=ccs.h, v=ccs.n: e.wait_ge(h, v))
            p2a_ch[0] = ch
        wait("sync", s_t, p2a_rd[par])
        p2a_tok[par] = ld(XN2[par][:],
                          agx_win(w_, sub_ * 512, (sub_ + 1) * 512)
                          .rearrange("(kc p) m -> p kc m", p=128))

    p2a_load(0)
    for wi_, w in enumerate(P2A_ORDER):
        for sub in range(WIN // 512):
            stp = wi_ * (WIN // 512) + sub
            par = stp % 2
            xnc = XN2[par]
            if stp + 1 < NW * (WIN // 512):
                p2a_load(stp + 1)
            wait_tok("tensor", p2a_tok[par])
            wait("tensor", s_v)
            for wave in range(2):
                for i in range(2):
                    mbk = wave * 2 + i
                    pdst = pA0[:, i * 512:(i + 1) * 512]
                    for kc in range(8):
                        t_quiet(lambda e, mbk=mbk, kc=kc, p=pdst, xnc=xnc:
                                e.matmul(
                            p, wip[:, kc, mbk * 128:(mbk + 1) * 128],
                            xnc[:, kc, :], start=(kc == 0), stop=(kc == 7)))
                td = t_op(lambda e: e.nop())
                wait("vector", s_t, td)
                vv = None
                for i in range(2):
                    mbk = wave * 2 + i
                    pdst = pA0[:, i * 512:(i + 1) * 512]
                    if mbk < NDT:
                        vv = v_op(lambda e, mbk=mbk, sub=sub, p=pdst:
                                  e.tensor_copy(out=xiwf[:, mbk,
                                                         3 + sub * 512:
                                                         3 + (sub + 1) * 512],
                                                in_=p))
                    else:
                        vv = v_op(lambda e, mbk=mbk, sub=sub, p=pdst:
                                  e.tensor_copy(out=zw2[:, 0, mbk - NDT,
                                                        sub * 512:
                                                        (sub + 1) * 512],
                                                in_=p))
                wait("tensor", s_v, vv)
            p2a_rd[par] = t_op(lambda e: e.nop())
        wait("sync", s_v)
        wait("gpsimd", s_v)
        for dt_i in range(NDT):
            st(z_sp.ap()[dt_i * 128:(dt_i + 1) * 128, w * WIN:(w + 1) * WIN],
               zw2[:, 0, dt_i, :])
            gdma(xi_sp.ap()[dt_i * 128:(dt_i + 1) * 128,
                            w * WIN:(w + 1) * WIN],
                 xiwf[:, dt_i, 3:3 + WIN].bitcast(F32))
        wait_pool("sync", p_st)
        wait_pool("sync", p_gp)
        wait_pool("vector", p_st)
        wait_pool("vector", p_gp)
    fence()

    for g in range(NG):
        for wi in range(GRP):
            w = g * GRP + wi
            seq_first = (w % SEQWIN == 0)
            seq_last = (w % SEQWIN == SEQWIN - 1)
            for dt_i in range(NDT):
                c_lo = w * WIN - (0 if seq_first else 3)
                c_hi = (w + 1) * WIN + (0 if seq_last else 3)
                d_lo = 3 - (w * WIN - c_lo)
                ld(xist[:, dt_i, 0:c_hi - c_lo],
                   xi_sp.ap()[dt_i * 128:(dt_i + 1) * 128, c_lo:c_hi])
                if seq_first:
                    v_op(lambda e, dt_i=dt_i: e.tensor_copy(
                        out=xiwf[:, dt_i, 0:3], in_=zerf[:, 0:3]))
                if seq_last:
                    v_op(lambda e, dt_i=dt_i: e.tensor_copy(
                        out=xiwf[:, dt_i, 3 + WIN:6 + WIN], in_=zerf[:, 0:3]))
                wait_pool("vector", p_ld)
                wait("vector", s_s)
                wait("vector", s_t)
                v_op(lambda e, dt_i=dt_i, d_lo=d_lo, c_lo=c_lo, c_hi=c_hi:
                     e.tensor_copy(out=xiwf[:, dt_i, d_lo:d_lo + c_hi - c_lo],
                                   in_=xist[:, dt_i, 0:c_hi - c_lo]))
            vv = s_v.n
            wait_pool("tensor", p_ld)
            wait_pool("tensor", p_gp)
            wait("tensor", s_v)
            for d in "fb":
                xcd = xcw2
                for dt_i in range(NDT):
                    for half in range(WIN // 512):
                        for k in range(4):
                            off = (k if d == "f" else (6 - k)) + half * 512
                            t_quiet(lambda e, d=d, dt_i=dt_i, k=k, off=off,
                                    half=half:
                                    e.matmul(pB[:, half * 512:(half + 1) * 512],
                                             cdiag[d][:, (dt_i * 4 + k) * 128:
                                                      (dt_i * 4 + k + 1) * 128],
                                             xiwf[:, dt_i, off:off + 512],
                                             start=(k == 0), stop=(k == 3)))
                    td = t_op(lambda e: e.nop())
                    wait("scalar", s_t, td)
                    wait("scalar", s_v)
                    wait_pool("scalar", p_st)
                    sd = s_op(lambda e, d=d, dt_i=dt_i, xcd=xcd: e.activation(
                        out=xcd[:, dt_i, :], in_=pB[:, 0:WIN],
                        func=AF.Silu, bias=convb_c[d][:, dt_i:dt_i + 1]))
                    wait("tensor", s_s, sd)
                wait("sync", s_s)
                for dt_i in range(NDT):
                    st(xc_sp[d].ap()[dt_i * 128:(dt_i + 1) * 128,
                                     w * WIN:(w + 1) * WIN], xcd[:, dt_i, :])
                for sub in range(WIN // 512):
                    for dt_i in range(NDT):
                        t_quiet(lambda e, d=d, dt_i=dt_i, sub=sub, xcd=xcd:
                                e.matmul(pA1[0:XDB, :],
                                         xpjT_c[d][:, dt_i, :],
                                         xcd[:, dt_i,
                                             sub * 512:(sub + 1) * 512],
                                         start=(dt_i == 0),
                                         stop=(dt_i == NDT - 1)))
                    td = t_op(lambda e: e.nop())
                    wait("vector", s_t, td)
                    vv = v_op(lambda e, sub=sub: e.tensor_copy(
                        out=x96v[:, sub * 512:(sub + 1) * 512],
                        in_=pA1[0:XDB, :]))
                    wait("tensor", s_v, vv)
                wait("sync", s_v)
                r0 = 0 if d == "f" else XDB
                st(ar_in[g].ap()[r0:r0 + XDB, wi * WIN:(wi + 1) * WIN],
                   x96v[:])
            wait_pool("sync", p_st)
            wait("sync", s_t)
        wait_pool("gpsimd", p_st)
        cc_op(lambda e, g=g: e.collective_compute(
            "AllReduce", OP.add, replica_groups=[list(range(NCORES))],
            ins=[ar_in[g].ap().opt()], outs=[ar_out[g].ap().opt()]))
    fence()

    PS = {"evB": [0, 0], "evC": [0, 0], "bbc_rd": [0, 0], "cbc_rd": [0, 0],
          "rpr": [0, 0], "scv": [0, 0], "car": [0, 0], "qpr": [0, 0],
          "ynop": [0, 0], "fexp": [0, 0], "lnS": [0, 0], "sig": [0, 0],
          "uv": [0, 0], "ful": [0, 0], "ful_rd": [0, 0],
          "ff_tok": [None, None], "smix_tok": None, "szw_rd": 0,
          "chunk_toks": []}
    LNS = [lnsig, lnsigB]
    XCW = [xcw, xcw2]
    TOKS = {}

    def sdep(req=0):
        return max(req, s_s.n)

    def emit_bcast(p, n):
        s = n % 2
        wait("tensor", s_s, PS["evC"][(n - 1) % 2])
        for hh in range(2):
            t_quiet(lambda e, p=p, n=n, hh=hh: e.matmul(
                pA0[:, hh * 512:(hh + 1) * 512],
                nind[:, n * 128:(n + 1) * 128],
                bcrowv2[0:NS, p, hh * 512:(hh + 1) * 512],
                start=True, stop=True))
        tb = t_op(lambda e: e.nop())
        wait("scalar", s_t, tb)
        wait("scalar", s_v, PS["bbc_rd"][s])
        PS["evB"][s] = s_op(lambda e, s=s: e.activation(
            out=bbc2[:, s, :], in_=pA0[:], func=AF.Copy), dep=sdep())
        wait("tensor", s_s, PS["evB"][s])
        for hh in range(2):
            t_quiet(lambda e, p=p, n=n, hh=hh: e.matmul(
                pA0[:, hh * 512:(hh + 1) * 512],
                nind[:, n * 128:(n + 1) * 128],
                ccrowv2[0:NS, p, hh * 512:(hh + 1) * 512],
                start=True, stop=True))
        tc = t_op(lambda e: e.nop())
        wait("scalar", s_t, tc)
        wait("scalar", s_v, PS["cbc_rd"][s])
        PS["evC"][s] = s_op(lambda e, s=s: e.activation(
            out=cbc2[:, s, :], in_=pA0[:], func=AF.Copy), dep=sdep())

    def emit_loads(d, i, w_list, gate):
        w = w_list[i]
        p = i % 2
        for eng in ("gpsimd", "sync"):
            wait(eng, s_t)
            wait(eng, s_v)
            wait(eng, s_s)
        g = w // GRP
        wi = w % GRP
        rbase = 0 if d == "f" else XDB
        tk_t = [gdma(dtmv2[0:DTR, p, :],
                     ar_out[g].ap()[rbase:rbase + DTR,
                                    wi * WIN:(wi + 1) * WIN]),
                gdma(bcrowv2[0:NS, p, :],
                     ar_out[g].ap()[rbase + DTR:rbase + DTR + NS,
                                    wi * WIN:(wi + 1) * WIN]),
                gdma(ccrowv2[0:NS, p, :],
                     ar_out[g].ap()[rbase + DTR + NS:rbase + XDB,
                                    wi * WIN:(wi + 1) * WIN])]
        tk_v = []
        tk_s = []
        for dt_i in range(NDT):
            tk_v.append(ld(XCW[p][:, dt_i, :],
                           xc_sp[d].ap()[dt_i * 128:(dt_i + 1) * 128,
                                         w * WIN:(w + 1) * WIN]))
        if gate:
            for dt_i in range(NDT):
                tk_v.append(ld(ffw2[:, p, dt_i, :],
                               ff_sp.ap()[dt_i * 128:(dt_i + 1) * 128,
                                          w * WIN:(w + 1) * WIN]))
                tk_s.append(ld(zw2[:, p, dt_i, :],
                               z_sp.ap()[dt_i * 128:(dt_i + 1) * 128,
                                         w * WIN:(w + 1) * WIN]))
        TOKS[p] = (tk_t, tk_v, tk_s)

    def emit_prologue(d, i, w_list):
        p = i % 2
        wait_tokens("tensor", TOKS[p][0])
        for dt_i in range(NDT):
            if dt_i == 0:
                wait("tensor", s_v, PS["ful"][1])
            else:
                wait("tensor", s_s, PS["sig"][0])
            for sub in range(2):
                t_quiet(lambda e, d=d, dt_i=dt_i, sub=sub, p=p: e.matmul(
                    pB[:, 1024 + sub * 512:1024 + (sub + 1) * 512],
                    dtwT_c[d][:, dt_i * 128:(dt_i + 1) * 128],
                    dtmv2[0:DTR, p, sub * 512:(sub + 1) * 512],
                    start=True, stop=True))
            td = t_op(lambda e: e.nop())
            wait("scalar", s_t, td)
            PS["sig"][dt_i] = s_op(lambda e, d=d, dt_i=dt_i: e.activation(
                out=sigw[:], in_=pB[:, 1024:2048], func=AF.Sigmoid,
                scale=-1.0, bias=negdtb_c[d][:, dt_i:dt_i + 1]))
            PS["lnS"][dt_i] = s_op(lambda e, p=p, dt_i=dt_i: e.activation(
                out=LNS[p][:, dt_i, :], in_=sigw[:], func=AF.Ln))
            wait("scalar", s_v, PS["scv"][dt_i])
            PS["fexp"][dt_i] = s_op(
                lambda e, d=d, p=p, dt_i=dt_i: e.activation(
                    out=fw2[:, dt_i, :], in_=LNS[p][:, dt_i, :], func=AF.Exp,
                    scale=aexp_c[d][:, dt_i * NS:dt_i * NS + 1]),
                dep=sdep(PS["lnS"][dt_i]))
            wait_tokens("vector", TOKS[p][1])
            wait("vector", s_s, PS["lnS"][dt_i])
            PS["uv"][dt_i] = v_sp(
                lambda e, p=p, dt_i=dt_i: e.scalar_tensor_tensor(
                    out=uw2[:, p, dt_i, :], in0=LNS[p][:, dt_i, :],
                    scalar=-1.0, in1=XCW[p][:, dt_i, :],
                    op0=OP.mult, op1=OP.mult),
                dep=PS["rpr"][dt_i])
        emit_bcast(p, 0)

    def emit_dtwork(d, i, n, reset):
        p = i % 2
        s = n % 2
        if n + 1 < NS:
            emit_bcast(p, n + 1)
        for dt_i in range(NDT):
            wait("vector", s_s, PS["evB"][s])
            PS["rpr"][dt_i] = v_op(
                lambda e, p=p, s=s, dt_i=dt_i: e.tensor_tensor(
                    out=rw2[:, dt_i, :], in0=uw2[:, p, dt_i, :],
                    in1=bbc2[:, s, :], op=OP.mult),
                dep=max(PS["scv"][dt_i], PS["uv"][dt_i]))
            PS["bbc_rd"][s] = PS["rpr"][dt_i]
            wait("vector", s_s, PS["fexp"][dt_i])
            ini = (0.0 if reset else
                   carry[d][:, dt_i * NS + n:dt_i * NS + n + 1])
            if d == "f":
                PS["scv"][dt_i] = v_sp(
                    lambda e, ini=ini, dt_i=dt_i: e.tensor_tensor_scan(
                        out=hw2[:, dt_i, :], data0=fw2[:, dt_i, :],
                        data1=rw2[:, dt_i, :], initial=ini,
                        op0=OP.mult, op1=OP.add),
                    dep=max(PS["rpr"][dt_i], PS["qpr"][dt_i],
                            PS["car"][dt_i]))
            else:
                PS["scv"][dt_i] = v_sp(
                    lambda e, ini=ini, dt_i=dt_i: e.tensor_tensor_scan(
                        out=hw2[:, dt_i, ::-1], data0=fw2[:, dt_i, ::-1],
                        data1=rw2[:, dt_i, ::-1], initial=ini,
                        op0=OP.mult, op1=OP.add),
                    dep=max(PS["rpr"][dt_i], PS["qpr"][dt_i],
                            PS["car"][dt_i]))
            cend = (WIN - 1) if d == "f" else 0
            PS["car"][dt_i] = v_op(
                lambda e, d=d, dt_i=dt_i, n=n, cend=cend: e.tensor_copy(
                    out=carry[d][:, dt_i * NS + n:dt_i * NS + n + 1],
                    in_=hw2[:, dt_i, cend:cend + 1]),
                dep=PS["scv"][dt_i])
            wait("vector", s_t, PS["ynop"][dt_i])
            wait("vector", s_s, PS["evC"][s])
            PS["qpr"][dt_i] = v_op(
                lambda e, s=s, dt_i=dt_i: e.tensor_tensor(
                    out=qw2[:, dt_i, :], in0=hw2[:, dt_i, :],
                    in1=cbc2[:, s, :], op=OP.mult),
                dep=PS["scv"][dt_i])
            PS["cbc_rd"][s] = PS["qpr"][dt_i]
            if n + 1 < NS:
                wait("scalar", s_v, PS["scv"][dt_i])
                PS["fexp"][dt_i] = s_op(
                    lambda e, d=d, p=p, dt_i=dt_i, n=n: e.activation(
                        out=fw2[:, dt_i, :], in_=LNS[p][:, dt_i, :],
                        func=AF.Exp,
                        scale=aexp_c[d][:, dt_i * NS + n + 1:
                                        dt_i * NS + n + 2]),
                    dep=sdep(PS["lnS"][dt_i]))
            wait("tensor", s_v,
                 max(PS["qpr"][dt_i], PS["ful"][dt_i] if n == 0 else 0))
            if n == 0 and dt_i == 1:
                wait("tensor", s_s, PS["sig"][1])
            for hh in range(2):
                t_quiet(lambda e, dt_i=dt_i, n=n, hh=hh: e.matmul(
                    pB[:, dt_i * WIN + hh * 512:dt_i * WIN + (hh + 1) * 512],
                    identb[:], qw2[:, dt_i, hh * 512:(hh + 1) * 512],
                    start=(n == 0), stop=(n == NS - 1)))
            PS["ynop"][dt_i] = t_op(lambda e: e.nop())

    def emit_tail(d, i, w_list, gate):
        w = w_list[i]
        p = i % 2
        for dt_i in range(NDT):
            wait("vector", s_t, PS["ynop"][dt_i])
            if PS["ff_tok"][dt_i] is not None:
                wait_tokens("vector", [PS["ff_tok"][dt_i]])
                PS["ff_tok"][dt_i] = None
            PS["ful"][dt_i] = v_sp(
                lambda e, d=d, p=p, dt_i=dt_i: e.scalar_tensor_tensor(
                    out=fullw[:, dt_i, :], in0=XCW[p][:, dt_i, :],
                    scalar=dvec_c[d][:, dt_i:dt_i + 1],
                    in1=pB[:, dt_i * WIN:(dt_i + 1) * WIN],
                    op0=OP.mult, op1=OP.add),
                dep=PS["ful_rd"][dt_i])
        if not gate:
            for dt_i in range(NDT):
                wait("sync", s_v, PS["ful"][dt_i])
                PS["ff_tok"][dt_i] = st(
                    ff_sp.ap()[dt_i * 128:(dt_i + 1) * 128,
                               w * WIN:(w + 1) * WIN], fullw[:, dt_i, :])
        else:
            c0 = w * WIN
            prev_mix = 0
            for dt_i in range(NDT):
                wait_tokens("scalar", [TOKS[p][2][dt_i]])
                wait("scalar", s_v, PS["szw_rd"])
                sd = s_op(lambda e, p=p, dt_i=dt_i: e.activation(
                    out=szw[:], in_=zw2[:, p, dt_i, :], func=AF.Silu),
                    dep=sdep())
                wait("vector", s_s, sd)
                if PS["smix_tok"] is not None:
                    wait_tokens("vector", [PS["smix_tok"]])
                va = v_op(lambda e, p=p, dt_i=dt_i: e.tensor_tensor(
                    out=smixw[:], in0=fullw[:, dt_i, :],
                    in1=ffw2[:, p, dt_i, :], op=OP.add),
                    dep=max(PS["ful"][dt_i], prev_mix))
                vb = v_sp(lambda e: e.scalar_tensor_tensor(
                    out=smixw[:], in0=smixw[:], scalar=0.5, in1=szw[:],
                    op0=OP.mult, op1=OP.mult), dep=va)
                prev_mix = vb
                PS["szw_rd"] = vb
                PS["ful_rd"][dt_i] = va
                wait("sync", s_v, vb)
                gch = w // (TC // WIN)
                gcol = (w % (TC // WIN)) * WIN
                PS["smix_tok"] = st(
                    ags_in.ap()[gch * DSL + dt_i * 128:
                                gch * DSL + (dt_i + 1) * 128,
                                gcol:gcol + WIN],
                    smixw[:])
                PS["chunk_toks"].append(PS["smix_tok"])

    def emit_scan_phase(d, w_list, gate):
        if gate:
            wait_pool("sync", p_st)
            wait_pool("gpsimd", p_st)
        emit_loads(d, 0, w_list, gate)
        emit_prologue(d, 0, w_list)
        for i, w in enumerate(w_list):
            if i + 1 < len(w_list):
                emit_loads(d, i + 1, w_list, gate)
            seq_w = w % SEQWIN
            reset = (seq_w == 0) if d == "f" else (seq_w == SEQWIN - 1)
            for n in range(NS):
                emit_dtwork(d, i, n, reset)
            emit_tail(d, i, w_list, gate)
            if gate and w % (TC // WIN) == 0:
                gch = w // (TC // WIN)
                wait_tokens("gpsimd", PS["chunk_toks"])
                PS["chunk_toks"] = []
                cc_op(lambda e, gch=gch: e.collective_compute(
                    "AllGather", OP.bypass,
                    replica_groups=[list(range(NCORES))],
                    ins=[ags_in.ap()[gch * DSL:(gch + 1) * DSL, :].opt()],
                    outs=[ags_out.ap()[gch * NCORES * DSL:
                                       (gch + 1) * NCORES * DSL, :].opt()]))
            if i + 1 < len(w_list):
                emit_prologue(d, i + 1, w_list)

    emit_scan_phase("f", list(range(NW)), gate=False)
    emit_scan_phase("b", list(range(NW - 1, -1, -1)), gate=True)

    for eng in ("sync",):
        wait(eng, s_t)
        wait(eng, s_v)
        wait(eng, s_s)
    for tt_i in range(TC // 128):
        l = ld(wq_st[:, 0:DM], xn1_s.ap()[tt_i * 128:(tt_i + 1) * 128, :])
        wait_tok("vector", l)
        wait_tok("scalar", l)
        quant_chain(DM, 128, wq_st[0:128, 0:DM], qout[0:128, 0:DM])
        for fb in range(8):
            vd = pe_transpose(qout[:, fb * 128:(fb + 1) * 128], 128,
                              tb2[0:128, 0:128])
            wait("sync", s_v, vd)
            st(xqT_s.ap()[fb * 128:(fb + 1) * 128,
                          tt_i * 128:(tt_i + 1) * 128], tb2[0:128, 0:128])
        wait("sync", s_v)
        wait_pool("vector", p_st)
    wait_pool("sync", p_st)
    for eng in ("sync",):
        wait(eng, s_t)
        wait(eng, s_v)
        wait(eng, s_s)
    wait("tensor", s_v)
    NTB4 = TC // 512
    NSTEP = NTB4 * 3 * 4
    TOTAL_DN = NTB4 * 4 * 22
    FB = {"xq_rd": [0, 0], "xq_tok": [None, None], "gu_last": 0,
          "gf_tok": None, "wgu_toks": None}
    DN = {"g": 0, "use": 0, "nop": {}, "tok": {}}

    def issue_wdm():
        g = DN["g"]
        DN["g"] += 1
        fc = g % 22
        sl = g % 4
        if g >= 4:
            wait("sync", s_t, DN["nop"].pop(g - 4))
        DN["tok"][g] = ld(
            wdm4[:, sl],
            agwd_out.ap().rearrange("(ob f) m -> f ob m", ob=8)
            [fc * 128:(fc + 1) * 128, :, :])

    def issue_wgu(si):
        f0 = sum(FSUBS[:si])
        fwd = FSUBS[si]
        wait("sync", s_t, FB["gu_last"])
        toks = []
        for (src_ag, coff) in ((agwg_out, 0), (agwu_out, fwd)):
            seg0 = f0
            while seg0 < f0 + fwd:
                blk = seg0 // FSH_GU
                b0 = seg0 % FSH_GU
                sl = min(FSH_GU - b0, f0 + fwd - seg0)
                toks.append(ld(
                    wgu[:, :, coff + seg0 - f0:coff + seg0 - f0 + sl],
                    src_ag.ap()[blk * DM:(blk + 1) * DM, b0:b0 + sl]
                    .rearrange("(kc p) m -> p kc m", p=128)))
                seg0 += sl
        FB["wgu_toks"] = toks

    def issue_xq(tt_i, par):
        wait("sync", s_t, FB["xq_rd"][par])
        FB["xq_tok"][par] = ld(
            xqTt2[:, par],
            xqT_s.ap()[:, tt_i * 128:(tt_i + 1) * 128]
            .rearrange("(kc p) m -> p kc m", p=128))

    for _ in range(4):
        issue_wdm()
    issue_wgu(0)
    issue_xq(0, 0)
    step = 0
    for tb4 in range(NTB4):
        f0 = 0
        for si, fwd in enumerate(FSUBS):
            wait_tokens("tensor", FB["wgu_toks"])
            for tq in range(4):
                tt_i = tb4 * 4 + tq
                par = step % 2
                wait_tokens("tensor", [FB["xq_tok"][par]])
                wait("tensor", s_v)
                wait("tensor", s_s)
                for fq in range(0, fwd, 512):
                    fe = min(fq + 512, fwd)
                    for kc in range(8):
                        t_quiet(lambda e, kc=kc, fq=fq, fe=fe, par=par:
                                e.matmul(pB[:, fq:fe],
                                         xqTt2[:, par, kc, :],
                                         wgu[:, kc, fq:fe],
                                         start=(kc == 0), stop=(kc == 7)))
                td = t_op(lambda e: e.nop())
                if step + 1 < NSTEP:
                    stp = step + 1
                    ntb4 = stp // 12
                    ntq = stp % 4
                    issue_xq(ntb4 * 4 + ntq, stp % 2)
                wait("scalar", s_t, td)
                wait("scalar", s_v)
                sd = s_op(lambda e, fwd=fwd: e.activation(
                    out=qout[:, 0:fwd], in_=pB[:, 0:fwd], func=AF.Silu))
                wait("tensor", s_s, sd)
                for fq in range(0, fwd, 512):
                    fe = min(fq + 512, fwd)
                    for kc in range(8):
                        t_quiet(lambda e, kc=kc, fq=fq, fe=fe, fwd=fwd,
                                par=par:
                                e.matmul(pB[:, fq:fe],
                                         xqTt2[:, par, kc, :],
                                         wgu[:, kc, fwd + fq:fwd + fe],
                                         start=(kc == 0), stop=(kc == 7)))
                td2 = t_op(lambda e: e.nop())
                FB["xq_rd"][par] = td2
                if tq == 3:
                    FB["gu_last"] = td2
                wait("vector", s_t, td2)
                wait("vector", s_s, sd)
                v_op(lambda e, fwd=fwd: e.tensor_copy(
                    out=hup[0:128, 0:fwd], in_=pB[:, 0:fwd]))
                v_op(lambda e, f0=f0, fwd=fwd, tq=tq: e.tensor_tensor(
                    out=hgt4[:, tq, f0:f0 + fwd], in0=qout[:, 0:fwd],
                    in1=hup[0:128, 0:fwd], op=OP.mult))
                step += 1
            if si + 1 < 3:
                issue_wgu(si + 1)
            elif tb4 + 1 < NTB4:
                issue_wgu(0)
            f0 += fwd
        for tq in range(4):
            tt_i = tb4 * 4 + tq
            vv = v_op(lambda e, tq=tq: e.tensor_copy(out=wq_st[:],
                                                     in_=hgt4[:, tq, :]))
            wait("scalar", s_v, vv)
            quant_chain(FH, 128, wq_st[:], qout[:])
            for fc in range(22):
                pe_transpose(qout[:, fc * 128:(fc + 1) * 128], 128,
                             hqT[:, fc, :])
            wait("tensor", s_v)
            gu = 0
            for fc in range(22):
                gu = DN["use"]
                DN["use"] += 1
                wait_tokens("tensor", [DN["tok"].pop(gu)])
                for hh in range(2):
                    t_quiet(lambda e, fc=fc, hh=hh, gu=gu: e.matmul(
                        pB[:, hh * 512:(hh + 1) * 512], hqT[:, fc, :],
                        wdm4f[:, gu % 4, hh * 512:(hh + 1) * 512],
                        start=(fc == 0), stop=(fc == 21)))
                td = t_op(lambda e: e.nop())
                DN["nop"][gu] = td
                if DN["g"] < TOTAL_DN:
                    issue_wdm()
            wait("scalar", s_t, td)
            wait("scalar", s_v)
            if FB["gf_tok"] is not None:
                wait_tokens("scalar", [FB["gf_tok"]])
            sd = s_op(lambda e: e.activation(out=gfin[:], in_=pB[:, 0:DM],
                                             func=AF.Silu))
            wait("sync", s_s, sd)
            wait("tensor", s_s, sd)
            FB["gf_tok"] = st(g_s.ap()[tt_i * 128:(tt_i + 1) * 128, :],
                              gfin[:])
    fence()

    vbase = [None]

    def get_vbase(e):
        if vbase[0] is None:
            vbase[0] = nc.values_load(
                scid[0:1, 0:1], engines=[mb.EngineType.Activation],
                min_val=0, max_val=NCORES - 1,
                skip_runtime_bounds_check=True)
        return vbase[0]

    wop_toks = [gdma(wop16[:, kc, :], wopT_in[kc * 128:(kc + 1) * 128, :])
                for kc in range(16)]
    P5 = {"smv_rd": [0, 0], "vrd": [0, 0], "st_tok": [None, None],
          "ln_toks": [None, None], "smv_tok": [None, None]}

    def issue_smv(tb4):
        slx = tb4 % 2
        sl_sc = p_sc.next_slot()
        if sl_sc.n > 0:
            em("scalar", lambda e, h=sl_sc.h, v=sl_sc.n: e.wait_ge(h, v))
        wait("scalar", s_t, P5["smv_rd"][slx])

        def emit_smv(e, tb4=tb4, slx=slx, h=sl_sc.h):
            vb = get_vbase(e)
            e.dma_start(
                out=smv2[:, slx],
                in_=ags_out.ap()
                .rearrange("(u p) t -> p u t", p=128)
                [:, :, tb4 * 512:(tb4 + 1) * 512]
                [:, bass.ts(vb, NCORES * DSL // 128), :]).then_inc(h, 16)
        em("scalar", emit_smv)
        sl_sc.inc(16)
        P5["smv_tok"][slx] = (sl_sc, sl_sc.n)

    def issue_ln(tt_i):
        par = tt_i % 2
        wait("sync", s_v, P5["vrd"][par])
        P5["ln_toks"][par] = [
            ld(txn2[:, par, :], xn1_s.ap()[tt_i * 128:(tt_i + 1) * 128, :]),
            ld(gf2[:, par, :], g_s.ap()[tt_i * 128:(tt_i + 1) * 128, :]),
            ld(tx2[:, par, :], x_in[tt_i * 128:(tt_i + 1) * 128, :])]

    issue_smv(0)
    issue_ln(0)
    first_t = [True]
    for tb4 in range(NTB4):
        if tb4 + 1 < NTB4:
            issue_smv(tb4 + 1)
        for tq in range(4):
            tt_i = tb4 * 4 + tq
            par = tt_i % 2
            if tt_i + 1 < NTB4 * 4:
                issue_ln(tt_i + 1)
            if first_t[0]:
                wait_tokens("tensor", wop_toks)
                first_t[0] = False
            if tq == 0:
                wait_tok("tensor", P5["smv_tok"][tb4 % 2])
            wait("tensor", s_v, P5["vrd"][par])
            for kc in range(16):
                for hh in range(2):
                    t_quiet(lambda e, kc=kc, tq=tq, hh=hh, par=par,
                            slx=tb4 % 2: e.matmul(
                        pB[:, par * 1024 + hh * 512:
                           par * 1024 + (hh + 1) * 512],
                        smv2[:, slx, kc, tq * 128:(tq + 1) * 128],
                        wop16[:, kc, hh * 512:(hh + 1) * 512],
                        start=(kc == 0), stop=(kc == 15)))
            td = t_op(lambda e: e.nop())
            if tq == 3:
                P5["smv_rd"][tb4 % 2] = td
            wait("vector", s_t, td)
            wait_tokens("vector", P5["ln_toks"][par])
            if P5["st_tok"][par] is not None:
                wait_tokens("vector", [P5["st_tok"][par]])
            v_op(lambda e, par=par: e.tensor_tensor(
                out=sfin[:, par, :], in0=pB[:, par * 1024:(par + 1) * 1024],
                in1=txn2[:, par, :], op=OP.add))
            v_op(lambda e, par=par: e.tensor_tensor(
                out=sfin[:, par, :], in0=sfin[:, par, :],
                in1=gf2[:, par, :], op=OP.mult))
            vv = v_op(lambda e, par=par: e.tensor_tensor(
                out=sfin[:, par, :], in0=sfin[:, par, :],
                in1=tx2[:, par, :], op=OP.add))
            P5["vrd"][par] = vv
            wait("sync", s_v, vv)
            P5["st_tok"][par] = st(out_ext[tt_i * 128:(tt_i + 1) * 128, :],
                                   sfin[:, par, :])
    fence()

    with es, nc.Block() as blk:
        @blk.sync
        def _(e):
            for fn in streams["sync"]:
                fn(e)

        @blk.gpsimd
        def _(e):
            for fn in streams["gpsimd"]:
                fn(e)

        @blk.vector
        def _(e):
            for fn in streams["vector"]:
                fn(e)

        @blk.scalar
        def _(e):
            for fn in streams["scalar"]:
                fn(e)

        @blk.tensor
        def _(e):
            for fn in streams["tensor"]:
                fn(e)

    return nc


_NC_CACHE = None


def kernel(**inputs):
    global _NC_CACHE
    f32 = np.float32

    def A(x):
        return np.ascontiguousarray(np.asarray(x, dtype=f32))

    x_full = A(inputs["L0_attn_x"]).reshape(TT, DM)
    in_proj = A(inputs["L0_attn_in_proj_w"])
    out_proj = A(inputs["L0_attn_out_proj_w"])
    wopT = np.ascontiguousarray(out_proj.T)
    normw = A(inputs["L0_attn_norm_w"])
    wg = A(inputs["L0_ffn_w_gate"])
    wu = A(inputs["L0_ffn_w_up"])
    wd = A(inputs["L0_ffn_w_down"])

    per = {}
    for d, tag in (("f", "_f"), ("b", "_b")):
        per[f"convw_{d}"] = A(inputs[f"L0_attn_conv_w{tag}"]).reshape(DI, 4)
        per[f"convb_{d}"] = A(inputs[f"L0_attn_conv_b{tag}"])
        per[f"xpj_{d}"] = A(inputs[f"L0_attn_xproj_w{tag}"])
        per[f"dtw_{d}"] = A(inputs[f"L0_attn_dtproj_w{tag}"])
        per[f"dtb_{d}"] = A(inputs[f"L0_attn_dtproj_b{tag}"])
        per[f"alog_{d}"] = A(inputs[f"L0_attn_A_log{tag}"])
        per[f"dvec_{d}"] = A(inputs[f"L0_attn_D{tag}"])

    in_maps = []
    for c in range(NCORES):
        dsl = slice(c * DSL, (c + 1) * DSL)
        rows = np.concatenate([in_proj[dsl], in_proj[DI + c * DSL:
                                                     DI + (c + 1) * DSL]], 0)
        m = {
            "x": x_full[c * TC:(c + 1) * TC],
            "normw": normw,
            "wipT": np.ascontiguousarray(rows.T),
            "wopT": wopT,
            "wg": wg[c * FSH_GU:(c + 1) * FSH_GU],
            "wu": wu[c * FSH_GU:(c + 1) * FSH_GU],
            "wd": wd[c * FSH_D:(c + 1) * FSH_D],
            "cid": np.array([[c]], dtype=np.uint32),
        }
        for d in "fb":
            m[f"convw_{d}"] = per[f"convw_{d}"][dsl]
            m[f"convb_{d}"] = per[f"convb_{d}"][dsl]
            m[f"xpjT_{d}"] = np.ascontiguousarray(per[f"xpj_{d}"][:, dsl].T)
            m[f"dtwT_{d}"] = np.ascontiguousarray(per[f"dtw_{d}"][dsl].T)
            m[f"dtb_{d}"] = per[f"dtb_{d}"][dsl]
            m[f"alog_{d}"] = per[f"alog_{d}"][dsl]
            m[f"dvec_{d}"] = per[f"dvec_{d}"][dsl]
        in_maps.append(m)

    if _NC_CACHE is None:
        _NC_CACHE = build_nc()
    res = run_bass_kernel_spmd(_NC_CACHE, in_maps, core_ids=list(range(NCORES)))
    out = np.concatenate([np.asarray(r["out"], dtype=np.float32)
                          for r in res.results], 0)
    return out.reshape(B, L, DM)


if __name__ == "__main__":
    import json
    rng = np.random.RandomState(0)
    nc = build_nc()
    ni = sum(1 for _ in nc.all_instructions())
    print("built; instructions:", ni)
